# version 15
# baseline (speedup 1.0000x reference)
import numpy as np
import ml_dtypes
import concourse.bass as bass
import concourse.mybir as mybir
from concourse.bass_utils import run_bass_kernel_spmd
from contextlib import ExitStack

F32 = mybir.dt.float32
BF16 = mybir.dt.bfloat16
FP8 = mybir.dt.float8e4
AF = mybir.ActivationFunctionType
OP = mybir.AluOpType
AX = mybir.AxisListType

H, D, C, CK, T, I, NI = 16, 128, 2048, 16, 4096, 6144, 48
BLK = 64
NPREF = 10
NZONE = 32
NPACK = NPREF + NZONE
M = NPACK * BLK
OWN = (NPREF + 16) * BLK
NOWN = 1024
GH, NG = 2, 8
SCALE = 1.0 / float(np.sqrt(D))
EPS = 1e-5
NVC = M // 128
TILE_W = [512, 512, 512, 512, 512, 128]
TILE_C0 = [0, 512, 1024, 1536, 2048, 2560]
INV_N = 1.0 / 2048.0


def _allowed(qb, kb):
    return 0 <= kb <= qb and ((qb - kb <= 16) or (kb < 2) or (kb % 4 == 0))


def _core_packed_blocks(qc):
    zone_lo = 16 * qc - 16
    if zone_lo > 0:
        pref = sorted(set([0, 1]) | set(k for k in range(0, zone_lo, 4)))
        pref = [b for b in pref if b < zone_lo]
    else:
        pref = []
    assert len(pref) <= NPREF
    slots = [(pref[s] if s < len(pref) else -1) for s in range(NPREF)]
    slots += [(zone_lo + z if zone_lo + z >= 0 else -1) for z in range(NZONE)]
    return slots, zone_lo


def _seg_table():
    tabs = []
    for itp in range(2):
        segs = []
        for j in range(5):
            segs.append(("pref", 128 * j, 128, j))
        if itp > 0:
            segs.append(("single", (NPREF + 1) * BLK, 64, 0))
            segs.append(("single", (NPREF + 0) * BLK, 64, 1))
            segs.append(("single", (NPREF + 4) * BLK, 64, 2))
        for j in range(12):
            segs.append(("win", (NPREF + 8 * itp + 2 * j) * BLK, 128, (itp, j)))
        tabs.append(segs)
    return tabs


SEGTAB = _seg_table()
NSEG0, NSEG1 = len(SEGTAB[0]), len(SEGTAB[1])


def _core_masks(qc):
    slots, zone_lo = _core_packed_blocks(qc)
    kbmulp = np.zeros((128, 5), np.float32)
    for j in range(5):
        for p in range(128):
            kbmulp[p, j] = 1.0 if slots[2 * j + p // 64] >= 0 else 0.0
    kbms = np.zeros((128, 4), np.float32)
    for ci, z in enumerate([1, 0, 4]):
        raw = zone_lo + z
        ok = 1.0 if (raw >= 0 and (raw < 2 or raw % 4 == 0)) else 0.0
        if ci == 0:
            kbms[64:128, ci] = ok
        else:
            kbms[:64, ci] = ok
    winm = np.zeros((128, 192), np.float32)
    for itp in range(2):
        for j in range(12):
            for p in range(128):
                raw = zone_lo + 8 * itp + 2 * j + p // 64
                for qq in range(8):
                    qb = 16 * qc + 8 * itp + qq
                    winm[p, (itp * 12 + j) * 8 + qq] = float(_allowed(qb, raw))
    cov = np.zeros((64, 70), np.int32)
    for itp in range(2):
        for (kind, col0, npp, ref) in SEGTAB[itp]:
            poff = col0 % 128
            for qq in range(8):
                qb = 16 * qc + 8 * itp + qq
                for sl in range(npp // 64):
                    p = poff + sl * 64
                    raw = slots[(col0 + sl * 64) // 64]
                    if kind == "pref":
                        w = kbmulp[p, ref]
                    elif kind == "single":
                        w = kbms[p, ref]
                    else:
                        w = winm[p, (ref[0] * 12 + ref[1]) * 8 + qq]
                    if w and raw >= 0:
                        cov[qb, raw] += 1
    for qb in range(16 * qc, 16 * qc + 16):
        for kb in range(64):
            want = 1 if _allowed(qb, kb) else 0
            assert cov[qb, kb] == want, (qc, qb, kb, cov[qb, kb], want)
    return kbmulp, kbms, winm


def _core_tables(qc):
    slots, _ = _core_packed_blocks(qc)
    toks = np.full(M, -1, np.int64)
    for s, raw in enumerate(slots):
        if raw >= 0:
            toks[s * 64:(s + 1) * 64] = np.arange(raw * 64, raw * 64 + 64)
    inv = 1.0 / (10000.0 ** (np.arange(0, D, 2, dtype=np.float64) / D))
    pos = np.where(toks < 0, 0, toks).astype(np.float64)
    emb = np.concatenate([np.outer(inv, pos)] * 2, 0)
    cos = np.cos(emb).astype(np.float32)
    sin = np.sin(emb).astype(np.float32)
    cos[:, toks < 0] = 1.0
    sin[:, toks < 0] = 0.0
    ss = sin.copy()
    ss[:64] *= -1.0
    return toks, cos, ss


class Emit:
    ENGINES = ("tensor", "scalar", "vector", "gpsimd", "sync")

    def __init__(self):
        self.events = []
        self.cnt = {}
        self.phase_first = {}
        self.barrier_waits = []

    def begin_phase(self):
        self.barrier_waits = [(s, v) for (s, v) in self.cnt.items() if v > 0]
        self.phase_first = {e: True for e in self.ENGINES}

    def op(self, engine, fn, waits=(), inc=None, inc_n=1):
        waits = [w for w in waits if w is not None]
        if self.phase_first.get(engine):
            waits = self.barrier_waits + waits
            self.phase_first[engine] = False
        incs = []
        tok = None
        if inc is not None:
            self.cnt[inc] = self.cnt.get(inc, 0) + inc_n
            incs.append((inc, inc_n))
            tok = (inc, self.cnt[inc])
        self.events.append((engine, fn, waits, incs))
        return tok

    def flush(self, nc, sems):
        events = self.events
        self.events = []
        with nc.Block() as blk:
            for eng_name in self.ENGINES:
                if not any(e[0] == eng_name for e in events):
                    continue
                def make(eng_name):
                    def body(eng):
                        for (e, fn, waits, incs) in events:
                            if e != eng_name:
                                continue
                            for (s, v) in waits:
                                eng.wait_ge(sems[s], v)
                            if fn is None:
                                continue
                            inst = fn(eng)
                            for (s, n) in incs:
                                inst.then_inc(sems[s], n)
                    return body
                getattr(blk, eng_name)(make(eng_name))


def build_nc():
    nc = bass.Bass("TRN2", target_bir_lowering=False)
    xkv = nc.dram_tensor("xkv", [C, M], F32, kind="ExternalInput")
    wqT = nc.dram_tensor("wqT", [C, C], F32, kind="ExternalInput")
    wkT = nc.dram_tensor("wkT", [C, C], F32, kind="ExternalInput")
    wvT = nc.dram_tensor("wvT", [C, C], F32, kind="ExternalInput")
    woT = nc.dram_tensor("woT", [C, C], F32, kind="ExternalInput")
    gw = nc.dram_tensor("gw", [C, I], F32, kind="ExternalInput")
    uw = nc.dram_tensor("uw", [C, I], F32, kind="ExternalInput")
    dw = nc.dram_tensor("dw", [I, C], F32, kind="ExternalInput")
    gst_d = nc.dram_tensor("gst", [128, NI], F32, kind="ExternalInput")
    ust_d = nc.dram_tensor("ust", [128, NI], F32, kind="ExternalInput")
    dst_d = nc.dram_tensor("dst", [128, CK], F32, kind="ExternalInput")
    tabc_d = nc.dram_tensor("tabc", [128, M], F32, kind="ExternalInput")
    tabs_d = nc.dram_tensor("tabs", [128, M], F32, kind="ExternalInput")
    kbmulp_d = nc.dram_tensor("kbmulp", [128, 5], F32, kind="ExternalInput")
    kbms_d = nc.dram_tensor("kbms", [128, 4], F32, kind="ExternalInput")
    winm_d = nc.dram_tensor("winm", [128, 192], F32, kind="ExternalInput")
    ones_d = nc.dram_tensor("ones", [128, 128], F32, kind="ExternalInput")
    ident_d = nc.dram_tensor("ident", [128, 128], F32, kind="ExternalInput")
    y = nc.dram_tensor("y", [NOWN, C], F32, kind="ExternalOutput")

    xkv_r = xkv.ap().rearrange("(ck p) t -> p ck t", p=128)
    wqT_r = wqT.ap().rearrange("(ck p) j -> p ck j", p=128)
    wkT_r = wkT.ap().rearrange("(ck p) j -> p ck j", p=128)
    wvT_r = wvT.ap().rearrange("(ck p) j -> p ck j", p=128)
    woT_r = woT.ap().rearrange("(ck p) j -> p ck j", p=128)
    gw_r = gw.ap().rearrange("(ck p) i -> p ck i", p=128)
    uw_r = uw.ap().rearrange("(ck p) i -> p ck i", p=128)
    dw_r = dw.ap().rearrange("(ik p) h -> p ik h", p=128)

    E = Emit()
    sem_names = ["ts", "ss", "vs", "gs", "d_xb0", "d_xb1", "d_w", "d_wo0", "d_wo1",
                 "d_gu0", "d_gu1", "d_ds0", "d_ds1", "d_misc", "d_out"]

    with ExitStack() as outer:
        ent = outer.enter_context
        actb = ent(nc.sbuf_tensor("actb", [128, H * 1024], BF16))
        gst = ent(nc.sbuf_tensor("gsts", [128, NI], F32))
        ust = ent(nc.sbuf_tensor("usts", [128, NI], F32))
        dst = ent(nc.sbuf_tensor("dsts", [128, CK], F32))
        kbmulp = ent(nc.sbuf_tensor("kbmulps", [128, 5], F32))
        kbms = ent(nc.sbuf_tensor("kbmss", [128, 4], F32))
        winm = ent(nc.sbuf_tensor("winms", [128, 192], F32))
        onesf = ent(nc.sbuf_tensor("onesf", [128, 128], F32))
        epsb = ent(nc.sbuf_tensor("epsb", [128, 1], F32))
        identb = ent(nc.sbuf_tensor("identb", [128, 128], BF16))
        sems = {n: ent(nc.semaphore(n)) for n in sem_names}
        wob = [ent(nc.sbuf_tensor("wob0", [128, CK * 256], BF16)), None]
        wo_last_use = [None, None]
        wo_loads = [0]
        def load_wo(band):
            E.op("gpsimd", lambda gp, band=band: gp.dma_start(
                wob[band % 2][:, :].rearrange("p (ck j) -> p ck j", j=256),
                woT_r[:, :, band * 256:(band + 1) * 256]),
                waits=[wo_last_use[band % 2]], inc=f"d_wo{band % 2}", inc_n=16)
            wo_loads[0] += 1

        with ExitStack() as ph:
            pe = ph.enter_context
            xbt = [pe(nc.sbuf_tensor(f"xbt{i}", [128, CK * 512], BF16)) for i in range(2)]
            wqg = pe(nc.sbuf_tensor("wqg", [128, CK * 256], BF16))
            wkg = pe(nc.sbuf_tensor("wkg", [128, CK * 256], BF16))
            wvg = pe(nc.sbuf_tensor("wvg", [128, CK * 256], BF16))
            kbg = pe(nc.sbuf_tensor("kbg", [128, GH * M], BF16))
            qbg = pe(nc.sbuf_tensor("qbg", [128, GH * 1024], BF16))
            vg = pe(nc.sbuf_tensor("vg", [128, NVC * 256], BF16))
            tabc = pe(nc.sbuf_tensor("tabc_s", [128, M], F32))
            tabs = pe(nc.sbuf_tensor("tabs_s", [128, M], F32))
            probs = [pe(nc.sbuf_tensor(f"probs{i}", [128, 1024], BF16)) for i in range(3)]
            probsm = [pe(nc.sbuf_tensor(f"probsm{i}", [128, 1024], BF16)) for i in range(3)]
            onesb = pe(nc.sbuf_tensor("onesb", [128, 128], BF16))
            winmfull = pe(nc.sbuf_tensor("winmfull", [128, 24 * 512], BF16))
            preffull = pe(nc.sbuf_tensor("preffull", [128, 5 * 512], BF16))
            snglfull = pe(nc.sbuf_tensor("snglfull", [128, 3 * 512], BF16))
            recipb = pe(nc.sbuf_tensor("recipb", [128, 1024], F32))
            sums_sb = pe(nc.sbuf_tensor("sums_sb", [128, 2 * 1024], F32))
            rt1 = pe(nc.sbuf_tensor("rt1", [128, 512], F32))
            rt2 = pe(nc.sbuf_tensor("rt2", [128, 512], F32))
            pp = [pe(nc.psum_tensor(f"pp{i}", [128, 512], F32)) for i in range(2)]
            sps = [pe(nc.psum_tensor(f"sps{i}", [128, 1024], F32)) for i in range(2)]
            cps = [pe(nc.psum_tensor(f"cps{i}", [128, 512], F32)) for i in range(2)]

            E.begin_phase()
            for (dstt, src) in [(tabc, tabc_d), (tabs, tabs_d), (kbmulp, kbmulp_d),
                                (kbms, kbms_d), (winm, winm_d), (onesf, ones_d),
                                (gst, gst_d), (ust, ust_d), (dst, dst_d)]:
                E.op("sync", lambda s, d=dstt, so=src: s.dma_start(d[:, :], so.ap()),
                     inc="d_misc", inc_n=16)
            E.op("gpsimd", lambda g: g.dma_start(identb[:, :], ident_d.ap()),
                 inc="d_misc", inc_n=16)
            t_misc = ("d_misc", E.cnt["d_misc"])
            E.op("vector", lambda v: v.memset(epsb[:, :], EPS))
            for itp in range(2):
                for j in range(12):
                    moff = (itp * 12 + j) * 8
                    E.op("vector", lambda v, itp=itp, j=j, moff=moff: v.tensor_tensor(
                        winmfull[:, (itp * 12 + j) * 512:(itp * 12 + j + 1) * 512],
                        bass.AP(winm, moff, [[192, 128], [1, 8], [0, 64]]),
                        bass.AP(winm, moff, [[192, 128], [1, 8], [0, 64]]), OP.bypass),
                        waits=[t_misc] if (itp == 0 and j == 0) else ())
            for j in range(5):
                E.op("vector", lambda v, j=j: v.tensor_tensor(
                    preffull[:, 512 * j: 512 * (j + 1)],
                    bass.AP(kbmulp, j, [[5, 128], [0, 512]]),
                    bass.AP(kbmulp, j, [[5, 128], [0, 512]]), OP.bypass))
            for j in range(3):
                E.op("vector", lambda v, j=j: v.tensor_tensor(
                    snglfull[:, 512 * j: 512 * (j + 1)],
                    bass.AP(kbms, j, [[4, 128], [0, 512]]),
                    bass.AP(kbms, j, [[4, 128], [0, 512]]), OP.bypass))
            t_masks = E.op("vector", lambda v: v.engine_nop(), inc="vs")
            t_onesb = E.op("scalar", lambda sc: sc.copy(onesb[:, :], onesf[:, :]),
                           waits=[t_misc], inc="ss")
            sps_free = [None, None]
            sps_free[0] = E.op("vector", lambda v: v.memset(sps[0][:, :], 0.0),
                               waits=[], inc="vs")
            sps_free[1] = E.op("vector", lambda v: v.memset(sps[1][:, :], 0.0), inc="vs")

            xb_free = [None, None]
            xb_refills = [0, 0]
            xb_prefetched = {}
            xb_tok = [None, None]
            pp_free = [None, None]
            probs_free = [None, None, None]
            pm_free = [None, None, None]
            cps_free = [None, None]
            w_tok = [None]
            pcnt = [0]
            expb = [0]

            def group_weight_load(g, after_tok):
                waits = [after_tok]
                for (dstt, src) in [(wqg, wqT_r), (wkg, wkT_r), (wvg, wvT_r)]:
                    E.op("gpsimd",
                         lambda gp, d=dstt, so=src, gg=g: gp.dma_start(
                             d[:, :].rearrange("p (ck j) -> p ck j", j=256),
                             so[:, :, gg * 256:(gg + 1) * 256]),
                         waits=waits, inc="d_w", inc_n=16)
                    waits = []
                w_tok[0] = ("d_w", E.cnt["d_w"])

            def proj(lhs_fn, rhs_fn, out_w, first_waits):
                ps = pcnt[0] % 2
                pcnt[0] += 1
                waits = list(first_waits) + ([pp_free[ps]] if pp_free[ps] else [])
                tok = None
                for ci in range(CK):
                    tok = E.op("tensor",
                               lambda te, ci=ci, ps=ps, ow=out_w: te.matmul(
                                   pp[ps][:, 0:ow], lhs_fn(ci), rhs_fn(ci),
                                   start=(ci == 0), stop=(ci == CK - 1)),
                               waits=waits if ci == 0 else (),
                               inc="ts" if ci == CK - 1 else None)
                    waits = []
                return ps, tok

            def rope_evac(ps, mmtok, dst_t, dcol, src_c0, w):
                E.op("vector", lambda v: v.tensor_mul(
                    rt1[:, 0:w], pp[ps][:, 0:w], tabc[:, src_c0:src_c0 + w]),
                    waits=[mmtok, t_misc])
                E.op("vector", lambda v: v.tensor_mul(
                    rt2[0:64, 0:w], pp[ps][64:128, 0:w], tabs[0:64, src_c0:src_c0 + w]))
                E.op("vector", lambda v: v.tensor_mul(
                    rt2[64:128, 0:w], pp[ps][0:64, 0:w], tabs[64:128, src_c0:src_c0 + w]))
                return E.op("vector", lambda v: v.tensor_add(
                    dst_t[:, dcol:dcol + w], rt1[:, 0:w], rt2[:, 0:w]), inc="vs")

            group_weight_load(0, None)
            for g in range(NG):
                g_w_tok = w_tok[0]
                rope_done_h = [None, None]
                q_done_h = [None, None]
                vproj_done = None
                for t in range(6):
                    w_t, c0 = TILE_W[t], TILE_C0[t]
                    st = t % 2
                    if (g, t) in xb_prefetched:
                        xb_tok[st] = xb_prefetched.pop((g, t))
                    else:
                        E.op("gpsimd",
                             lambda gp, st=st, c0=c0, w_t=w_t: gp.dma_start(
                                 xbt[st][:, :].rearrange("p (ck t) -> p ck t", t=512)[:, :, 0:w_t],
                                 xkv_r[:, :, c0:c0 + w_t]),
                             waits=[xb_free[st]], inc=f"d_xb{st}", inc_n=16)
                        xb_refills[st] += 1
                        xb_tok[st] = (f"d_xb{st}", 16 * xb_refills[st])
                    for l in range(GH):
                        ps, tok = proj(
                            lambda ci, l=l: wkg[:, 256 * ci + 128 * l: 256 * ci + 128 * (l + 1)],
                            lambda ci, st=st, w_t=w_t: xbt[st][:, 512 * ci: 512 * ci + w_t],
                            w_t, [xb_tok[st], g_w_tok])
                        rope_done_h[l] = rope_evac(ps, tok, kbg, l * M + c0, c0, w_t)
                        pp_free[ps] = rope_done_h[l]
                    q_lo, q_hi = max(c0, OWN), c0 + w_t
                    if q_hi > q_lo:
                        off, qw = q_lo - c0, q_hi - q_lo
                        for l in range(GH):
                            ps, tok = proj(
                                lambda ci, l=l: wqg[:, 256 * ci + 128 * l: 256 * ci + 128 * (l + 1)],
                                lambda ci, st=st, off=off, qw=qw: xbt[st][:, 512 * ci + off: 512 * ci + off + qw],
                                qw, [xb_tok[st], g_w_tok])
                            q_done_h[l] = rope_evac(ps, tok, qbg, l * 1024 + (q_lo - OWN), q_lo, qw)
                            pp_free[ps] = q_done_h[l]
                    for s in range(w_t // 128):
                        r = c0 // 128 + s
                        ps, tok = proj(
                            lambda ci, st=st, s=s: xbt[st][:, 512 * ci + 128 * s: 512 * ci + 128 * (s + 1)],
                            lambda ci: wvg[:, 256 * ci: 256 * (ci + 1)],
                            256, [xb_tok[st], g_w_tok])
                        vproj_done = E.op("scalar", lambda sc, ps=ps, r=r: sc.copy(
                            vg[:, 256 * r: 256 * (r + 1)], pp[ps][:, 0:256]),
                            waits=[tok], inc="ss")
                        pp_free[ps] = vproj_done
                    xb_free[st] = ("ts", E.cnt["ts"])
                if g + 1 < NG:
                    group_weight_load(g + 1, ("ts", E.cnt["ts"]))
                    for t in range(2):
                        st = t % 2
                        E.op("gpsimd",
                             lambda gp, st=st, c0=TILE_C0[t], w_t=TILE_W[t]: gp.dma_start(
                                 xbt[st][:, :].rearrange("p (ck t) -> p ck t", t=512)[:, :, 0:w_t],
                                 xkv_r[:, :, c0:c0 + w_t]),
                             waits=[xb_free[st]], inc=f"d_xb{st}", inc_n=16)
                        xb_refills[st] += 1
                        xb_prefetched[(g + 1, t)] = (f"d_xb{st}", 16 * xb_refills[st])

                head_ev = {}
                for l in range(GH):
                    h = GH * g + l
                    hs = h % 2
                    sums_last = None
                    batches = []
                    for itp in range(2):
                        segs = SEGTAB[itp]
                        for bi, b0 in enumerate(range(0, len(segs), 2)):
                            batches.append((itp, bi, segs[b0:b0 + 2], b0 + 2 >= len(segs)))
                    pend_q = []
                    for bk in range(len(batches) + 2):
                        if bk < len(batches):
                            (itp, bi, segs_b, last_of_it) = batches[bk]
                            sp = expb[0] % 2
                            expb[0] += 1
                            pslot = expb[0] % 3
                            sc_waits = [sps_free[sp]]
                            if bk == 0:
                                sc_waits += [rope_done_h[l], q_done_h[l], t_masks]
                            last_mm = None
                            for si, (kind, col0, npp, ref) in enumerate(segs_b):
                                poff = col0 % 128
                                last_mm = E.op("tensor",
                                    lambda te, si=si, col0=col0, npp=npp, poff=poff, l=l, itp=itp, sp=sp: te.matmul(
                                        sps[sp][poff:poff + npp, 512 * si: 512 * si + 512],
                                        kbg[:, l * M + col0: l * M + col0 + npp],
                                        qbg[:, l * 1024 + itp * 512: l * 1024 + (itp + 1) * 512],
                                        start=True, stop=True),
                                    waits=sc_waits if si == 0 else (), inc="ts")
                                sc_waits = []
                            fd = 512 * len(segs_b)
                            exp_waits = [w for w in (last_mm, probs_free[pslot]) if w]
                            etok = E.op("scalar",
                                lambda sc, fd=fd, sp=sp, pslot=pslot: sc.activation(
                                    probs[pslot][:, 0:fd], sps[sp][:, 0:fd],
                                    AF.Exp, scale=SCALE),
                                waits=exp_waits, inc="ss")
                            sps_free[sp] = etok
                            pend_q.append((itp, bi, segs_b, last_of_it, pslot, etok))
                        if (len(pend_q) > 2) or (bk >= len(batches) and pend_q):
                            (pit, pbi, psegs, plast, pslot, petok) = pend_q.pop(0)
                            cs = (2 * h + pit) % 2
                            ssl = (2 * h + pit) % 2
                            for si, (kind, col0, npp, ref) in enumerate(psegs):
                                pcol = 512 * si
                                poff = col0 % 128
                                if kind == "pref":
                                    msrc, mo = preffull, 512 * ref
                                elif kind == "single":
                                    msrc, mo = snglfull, 512 * ref
                                else:
                                    msrc, mo = winmfull, 512 * (ref[0] * 12 + ref[1])
                                mw = [petok, pm_free[pslot]] if si == 0 else [petok]
                                mtok = E.op("vector", lambda v, pcol=pcol, msrc=msrc, mo=mo, pslot=pslot: v.tensor_tensor(
                                    probsm[pslot][:, pcol:pcol + 512], probs[pslot][:, pcol:pcol + 512],
                                    msrc[:, mo:mo + 512], OP.mult),
                                    waits=[w for w in mw if w], inc="vs")
                                cw = [mtok]
                                first_ctx = (pbi == 0 and si == 0)
                                if first_ctx:
                                    cw.append(vproj_done)
                                    if cps_free[cs] is not None:
                                        cw.append(cps_free[cs])
                                r0 = col0 // 128
                                is_last = plast and (si == len(psegs) - 1)
                                ctx_tok = E.op("tensor",
                                    lambda te, r0=r0, poff=poff, npp=npp, l=l, pcol=pcol, cs=cs,
                                           fc=first_ctx, st_=is_last, pslot=pslot: te.matmul(
                                        cps[cs][:, 0:512],
                                        vg[poff:poff + npp, 256 * r0 + 128 * l: 256 * r0 + 128 * (l + 1)],
                                        probsm[pslot][poff:poff + npp, pcol:pcol + 512],
                                        start=fc, stop=st_),
                                    waits=cw, inc="ts")
                                smw = [t_onesb] if (first_ctx and h == 0) else []
                                if first_ctx and pp_free[ssl] is not None:
                                    smw.append(pp_free[ssl])
                                sum_tok = E.op("tensor",
                                    lambda te, poff=poff, npp=npp, pcol=pcol, ssl=ssl,
                                           fc=first_ctx, st_=is_last, pslot=pslot: te.matmul(
                                        pp[ssl][:, 0:512],
                                        onesb[poff:poff + npp, 0:128],
                                        probsm[pslot][poff:poff + npp, pcol:pcol + 512],
                                        start=fc, stop=st_),
                                    waits=smw, inc="ts")
                            probs_free[pslot] = ("vs", E.cnt["vs"])
                            pm_free[pslot] = ("ts", E.cnt["ts"])
                            if plast:
                                ev = E.op("vector", lambda v, h=h, it=pit, cs=cs: v.tensor_copy(
                                    actb[:, 1024 * h + 512 * it: 1024 * h + 512 * (it + 1)],
                                    cps[cs][:, 0:512]), waits=[ctx_tok], inc="vs")
                                cps_free[cs] = ev
                                stk = E.op("vector", lambda v, l=l, it=pit, ssl=ssl: v.tensor_copy(
                                    sums_sb[:, 1024 * l + 512 * it: 1024 * l + 512 * (it + 1)],
                                    pp[ssl][:, 0:512]),
                                    waits=[sum_tok], inc="vs")
                                pp_free[ssl] = stk
                                head_ev[(l, pit)] = ev
                for l in range(GH):
                    h = GH * g + l
                    for itp in range(2):
                        E.op("vector", lambda v, l=l, itp=itp: v.reciprocal(
                            recipb[:, 512 * itp: 512 * (itp + 1)],
                            sums_sb[:, 1024 * l + 512 * itp: 1024 * l + 512 * (itp + 1)]))
                    E.op("vector", lambda v, h=h: v.tensor_mul(
                        actb[:, 1024 * h: 1024 * (h + 1)],
                        actb[:, 1024 * h: 1024 * (h + 1)], recipb[:, :]),
                        waits=[head_ev[(l, 0)], head_ev[(l, 1)]], inc="vs")
            load_wo(0)
            E.flush(nc, sems)

        phBC = outer.enter_context(ExitStack())
        peBC = phBC.enter_context
        wob[1] = peBC(nc.sbuf_tensor("wob1", [128, CK * 256], BF16))
        gsl = [peBC(nc.sbuf_tensor(f"gsl{i}", [128, CK * 256], BF16)) for i in range(2)]
        usl = [peBC(nc.sbuf_tensor(f"usl{i}", [128, CK * 256], BF16)) for i in range(2)]
        g8 = [peBC(nc.sbuf_tensor(f"g8_{i}", [128, CK * 256], FP8)) for i in range(2)]
        u8 = [peBC(nc.sbuf_tensor(f"u8_{i}", [128, CK * 256], FP8)) for i in range(2)]
        gu_last_use = [None, None]
        def load_gu(band):
            slot = band % 2
            sem = f"d_gu{slot}"
            E.op("gpsimd", lambda gp, band=band, slot=slot: gp.dma_start(
                g8[slot][:, :].rearrange("p (ck i) -> p ck i", i=256),
                gw_r[:, :, band * 256:(band + 1) * 256]),
                waits=[gu_last_use[slot]], inc=sem, inc_n=16)
            t_u8 = E.op("gpsimd", lambda gp, band=band, slot=slot: gp.dma_start(
                u8[slot][:, :].rearrange("p (ck i) -> p ck i", i=256),
                uw_r[:, :, band * 256:(band + 1) * 256]), inc=sem, inc_n=16)
            E.op("gpsimd", lambda gp, slot=slot: gp.dma_start(
                gsl[slot][:, :], g8[slot][:, :]), waits=[t_u8], inc=sem, inc_n=16)
            E.op("gpsimd", lambda gp, slot=slot: gp.dma_start(
                usl[slot][:, :], u8[slot][:, :]), inc=sem, inc_n=16)
            return (sem, E.cnt[sem])

        with ExitStack() as ph:
            pe = ph.enter_context
            xrest = pe(nc.sbuf_tensor("xrest", [128, CK * 1024], F32))
            sqt = [pe(nc.sbuf_tensor(f"sqt{i}", [128, 1024], F32)) for i in range(2)]
            meanb = pe(nc.sbuf_tensor("meanb", [128, 1024], F32))
            varb = pe(nc.sbuf_tensor("varb", [128, 1024], F32))
            rstdb = pe(nc.sbuf_tensor("rstdb", [128, 1024], F32))
            tmpn = pe(nc.sbuf_tensor("tmpn", [128, 1024], F32))
            ppB = [pe(nc.psum_tensor(f"ppB{i}", [128, 512], F32)) for i in range(3)]
            st1 = pe(nc.psum_tensor("st1", [128, 1024], F32))
            st2 = pe(nc.psum_tensor("st2", [128, 1024], F32))

            E.begin_phase()
            load_wo(1)
            gu_tok = [None, None]
            gu_tok[0] = load_gu(0)
            gu_tok[1] = load_gu(1)
            E.op("sync", lambda s: s.dma_start(
                xrest[:, :].rearrange("p (ck t) -> p ck t", t=1024),
                xkv_r[:, :, OWN:OWN + 1024]), inc="d_misc", inc_n=16)
            t_xres = ("d_misc", E.cnt["d_misc"])

            ppB_free = [None, None, None]
            pcntB = 0
            sq_used = [None, None]
            s1_tok = s2_tok = None
            for band in range(8):
                slot = band % 2
                band_dma_target = (f"d_wo{band % 2}", 16 * (band // 2 + 1))
                for j4 in range(2):
                    j = 2 * band + j4
                    ev_j = []
                    for tt in range(2):
                        ps = pcntB % 3
                        pcntB += 1
                        waits = [band_dma_target, ppB_free[ps]]
                        mmtok = None
                        for ci in range(CK):
                            mmtok = E.op("tensor", lambda te, ci=ci, j4=j4, tt=tt, ps=ps, slot=slot: te.matmul(
                                ppB[ps][:, 0:512],
                                wob[slot][:, 256 * ci + 128 * j4: 256 * ci + 128 * (j4 + 1)],
                                actb[:, 1024 * ci + 512 * tt: 1024 * ci + 512 * (tt + 1)],
                                start=(ci == 0), stop=(ci == CK - 1)),
                                waits=waits if ci == 0 else (), inc="ts" if ci == CK - 1 else None)
                            waits = []
                        wo_last_use[slot] = mmtok
                        ev = E.op("vector", lambda v, j=j, tt=tt, ps=ps: v.tensor_add(
                            xrest[:, 1024 * j + 512 * tt: 1024 * j + 512 * (tt + 1)],
                            ppB[ps][:, 0:512],
                            xrest[:, 1024 * j + 512 * tt: 1024 * j + 512 * (tt + 1)]),
                            waits=[mmtok, t_xres], inc="vs")
                        ppB_free[ps] = ev
                        ev_j.append(ev)
                    ci = j
                    sl = ci % 2
                    sqtok = E.op("scalar", lambda sc, ci=ci, sl=sl: sc.activation(
                        sqt[sl][:, :], xrest[:, 1024 * ci: 1024 * (ci + 1)], AF.Square),
                        waits=[ev_j[0], ev_j[1], sq_used[sl]], inc="ss")
                    for half in range(2):
                        s1_tok = E.op("tensor", lambda te, ci=ci, half=half: te.matmul(
                            st1[:, 512 * half: 512 * (half + 1)], onesf[:, :],
                            xrest[:, 1024 * ci + 512 * half: 1024 * ci + 512 * (half + 1)],
                            start=(ci == 0), stop=(ci == CK - 1)),
                            waits=[ev_j[half]], inc="ts")
                    for half in range(2):
                        s2_tok = E.op("tensor", lambda te, ci=ci, half=half, sl=sl: te.matmul(
                            st2[:, 512 * half: 512 * (half + 1)], onesf[:, :],
                            sqt[sl][:, 512 * half: 512 * (half + 1)],
                            start=(ci == 0), stop=(ci == CK - 1)),
                            waits=[sqtok] if half == 0 else (), inc="ts")
                    sq_used[sl] = ("ts", E.cnt["ts"])
                if band + 2 < 8:
                    load_wo(band + 2)
            E.op("vector", lambda v: v.tensor_scalar(
                meanb[:, :], st1[:, :], INV_N, None, OP.mult), waits=[s1_tok])
            E.op("vector", lambda v: v.tensor_scalar(
                varb[:, :], st2[:, :], INV_N, None, OP.mult), waits=[s2_tok])
            E.op("vector", lambda v: v.tensor_mul(tmpn[:, :], meanb[:, :], meanb[:, :]))
            vtok = E.op("vector", lambda v: v.tensor_sub(varb[:, :], varb[:, :], tmpn[:, :]),
                        inc="vs")
            sdtok = E.op("scalar", lambda sc: sc.activation(
                rstdb[:, :], varb[:, :], AF.Sqrt, bias=epsb[:, 0:1]), waits=[vtok], inc="ss")
            E.op("vector", lambda v: v.reciprocal(rstdb[:, :], rstdb[:, :]), waits=[sdtok])
            for ci in range(CK):
                E.op("vector", lambda v, ci=ci: v.tensor_sub(
                    tmpn[:, :], xrest[:, 1024 * ci: 1024 * (ci + 1)], meanb[:, :]))
                E.op("vector", lambda v, ci=ci: v.tensor_mul(
                    actb[:, 1024 * ci: 1024 * (ci + 1)], tmpn[:, :], rstdb[:, :]),
                    inc="vs")
            E.flush(nc, sems)

        with ExitStack() as ph:
            pe = ph.enter_context
            fuse = pe(nc.sbuf_tensor("fuse", [128, NI * 1024], BF16))
            phC = ph.enter_context(ExitStack())
            peC = phC.enter_context
            gtile = [peC(nc.sbuf_tensor(f"gtile{i}", [128, 1024], BF16)) for i in range(2)]
            utile = [peC(nc.sbuf_tensor(f"utile{i}", [128, 1024], BF16)) for i in range(2)]
            pg = [peC(nc.psum_tensor(f"pg{i}", [128, 512], F32)) for i in range(2)]
            pu = [peC(nc.psum_tensor(f"pu{i}", [128, 512], F32)) for i in range(2)]

            E.begin_phase()
            pg_free = [None, None]
            pu_free = [None, None]
            gt_free = [None, None]
            for band in range(24):
                slot = band % 2
                for i4 in range(2):
                    i = 2 * band + i4
                    isl = i % 2
                    gdone = udone = None
                    for tt in range(2):
                        ps = (2 * i + tt) % 2
                        waits = [gu_tok[slot], pg_free[ps]]
                        mmtok = None
                        for ci in range(CK):
                            mmtok = E.op("tensor", lambda te, ci=ci, i4=i4, tt=tt, ps=ps, slot=slot: te.matmul(
                                pg[ps][:, 0:512],
                                gsl[slot][:, 256 * ci + 128 * i4: 256 * ci + 128 * (i4 + 1)],
                                actb[:, 1024 * ci + 512 * tt: 1024 * ci + 512 * (tt + 1)],
                                start=(ci == 0), stop=(ci == CK - 1)),
                                waits=waits if ci == 0 else (), inc="ts" if ci == CK - 1 else None)
                            waits = []
                        gdone = E.op("scalar", lambda sc, i=i, tt=tt, ps=ps, isl=isl: sc.activation(
                            gtile[isl][:, 512 * tt: 512 * (tt + 1)], pg[ps][:, 0:512],
                            AF.Silu, scale=gst[:, i:i + 1]),
                            waits=[mmtok, gt_free[isl] if tt == 0 else None], inc="ss")
                        pg_free[ps] = gdone
                    for tt in range(2):
                        ps = (2 * i + tt) % 2
                        waits = [gu_tok[slot], pu_free[ps]]
                        mmtok = None
                        for ci in range(CK):
                            mmtok = E.op("tensor", lambda te, ci=ci, i4=i4, tt=tt, ps=ps, slot=slot: te.matmul(
                                pu[ps][:, 0:512],
                                usl[slot][:, 256 * ci + 128 * i4: 256 * ci + 128 * (i4 + 1)],
                                actb[:, 1024 * ci + 512 * tt: 1024 * ci + 512 * (tt + 1)],
                                start=(ci == 0), stop=(ci == CK - 1)),
                                waits=waits if ci == 0 else (), inc="ts" if ci == CK - 1 else None)
                            waits = []
                        udone = E.op("scalar", lambda sc, i=i, tt=tt, ps=ps, isl=isl: sc.activation(
                            utile[isl][:, 512 * tt: 512 * (tt + 1)], pu[ps][:, 0:512],
                            AF.Copy, scale=ust[:, i:i + 1]),
                            waits=[mmtok, gt_free[isl] if tt == 0 else None], inc="ss")
                        pu_free[ps] = udone
                    gt_free[isl] = E.op("vector", lambda v, i=i, isl=isl: v.tensor_mul(
                        fuse[:, 1024 * i: 1024 * (i + 1)], gtile[isl][:, :], utile[isl][:, :]),
                        waits=[gdone, udone], inc="vs")
                gu_last_use[slot] = ("ts", E.cnt["ts"])
                if band + 2 < 24:
                    gu_tok[slot] = load_gu(band + 2)
            E.flush(nc, sems)
            phC.close()

            with ExitStack() as ph2:
                pe2 = ph2.enter_context
                dsl = [pe2(nc.sbuf_tensor(f"dsl{i}", [128, 4 * 512], BF16)) for i in range(2)]
                d8 = [pe2(nc.sbuf_tensor(f"d8_{i}", [128, 4 * 512], FP8)) for i in range(2)]
                pd = [pe2(nc.psum_tensor(f"pd{i}", [128, 512], F32)) for i in range(8)]

                E.begin_phase()
                ds_last_use = [None, None]
                def load_ds(band, ig):
                    slot = (band * 12 + ig) % 2
                    sem = f"d_ds{slot}"
                    t_d8 = E.op("gpsimd", lambda gp, band=band, ig=ig, slot=slot: gp.dma_start(
                        d8[slot][:, :].rearrange("p (ik h) -> p ik h", h=512),
                        dw_r[:, 4 * ig: 4 * ig + 4, 512 * band: 512 * (band + 1)]),
                        waits=[ds_last_use[slot]], inc=sem, inc_n=16)
                    E.op("gpsimd", lambda gp, slot=slot: gp.dma_start(
                        dsl[slot][:, :], d8[slot][:, :]), waits=[t_d8], inc=sem, inc_n=16)
                    return (sem, E.cnt[sem])
                pd_free = [None] * 8
                for band in range(4):
                    ds_tok = {}
                    ds_tok[0] = load_ds(band, 0)
                    ds_tok[1] = load_ds(band, 1)
                    for ig in range(12):
                        slot = (band * 12 + ig) % 2
                        for ii in range(4):
                            i = 4 * ig + ii
                            for hh in range(4):
                                h = 4 * band + hh
                                for tt in range(2):
                                    ps = 2 * hh + tt
                                    waits = []
                                    if i == 0:
                                        waits = [ds_tok[ig], pd_free[ps]]
                                    elif ii == 0:
                                        waits = [ds_tok[ig]]
                                    mmtok = E.op("tensor",
                                        lambda te, ii=ii, hh=hh, tt=tt, ps=ps, slot=slot, i=i: te.matmul(
                                            pd[ps][:, 0:512],
                                            dsl[slot][:, 512 * ii + 128 * hh: 512 * ii + 128 * (hh + 1)],
                                            fuse[:, 1024 * i + 512 * tt: 1024 * i + 512 * (tt + 1)],
                                            start=(i == 0), stop=(i == NI - 1)),
                                        waits=waits,
                                        inc="ts" if (i == NI - 1 or (ii == 3)) else None)
                        ds_last_use[slot] = ("ts", E.cnt["ts"])
                        if ig + 2 < 12:
                            ds_tok[ig + 2] = load_ds(band, ig + 2)
                    last_tok = ("ts", E.cnt["ts"])
                    for hh in range(4):
                        h = 4 * band + hh
                        for tt in range(2):
                            ps = 2 * hh + tt
                            pd_free[ps] = E.op("vector",
                                lambda v, h=h, tt=tt, ps=ps: v.scalar_tensor_tensor(
                                    actb[:, 1024 * h + 512 * tt: 1024 * h + 512 * (tt + 1)],
                                    pd[ps][:, 0:512], dst[:, h:h + 1],
                                    actb[:, 1024 * h + 512 * tt: 1024 * h + 512 * (tt + 1)],
                                    OP.mult, OP.add),
                                waits=[last_tok], inc="vs")
                E.flush(nc, sems)
        phBC.close()

        with ExitStack() as ph:
            pe = ph.enter_context
            tstage = [pe(nc.sbuf_tensor(f"tstage{i}", [128, C], F32)) for i in range(8)]
            ostage = [pe(nc.sbuf_tensor(f"ostage{i}", [128, C], F32)) for i in range(4)]
            trash = [pe(nc.sbuf_tensor(f"trash{i}", [128, C], F32)) for i in range(2)]
            srow = pe(nc.sbuf_tensor("srow", [128, 64], F32))
            ptr = [pe(nc.psum_tensor(f"ptr{i}", [128, 1024], BF16)) for i in range(4)]

            E.begin_phase()
            ptr_free = [None, None, None, None]
            qtok = None
            for t8 in range(8):
                cp_tok = None
                for ci in range(CK):
                    ps = ci % 4
                    trtok = E.op("tensor", lambda te, ci=ci, t8=t8, ps=ps: te.transpose(
                        ptr[ps][:, 0:128], actb[:, 1024 * ci + 128 * t8: 1024 * ci + 128 * (t8 + 1)],
                        identb[:, :]),
                        waits=[ptr_free[ps], t_misc if (t8 == 0 and ci == 0) else None],
                        inc="ts")
                    cp_tok = E.op("vector", lambda v, ci=ci, t8=t8, ps=ps: v.tensor_copy(
                        tstage[t8][:, 128 * ci: 128 * (ci + 1)], ptr[ps][:, 0:128]),
                        waits=[trtok], inc="vs")
                    ptr_free[ps] = cp_tok
                E.op("vector", lambda v, t8=t8: v.reduce_sum(
                    srow[:, t8:t8 + 1], tstage[t8][:, :], AX.X), waits=[cp_tok])
                E.op("vector", lambda v, t8=t8: v.tensor_mul(
                    trash[t8 % 2][:, :], tstage[t8][:, :], tstage[t8][:, :]))
                qtok = E.op("vector", lambda v, t8=t8: v.reduce_sum(
                    srow[:, 8 + t8:9 + t8], trash[t8 % 2][:, :], AX.X), inc="vs")
            t_m = E.op("vector", lambda v: v.tensor_scalar(
                srow[:, 16:24], srow[:, 0:8], INV_N, None, OP.mult), waits=[qtok], inc="vs")
            t_mq = E.op("vector", lambda v: v.tensor_scalar(
                srow[:, 24:32], srow[:, 8:16], INV_N, None, OP.mult), waits=[t_m], inc="vs")
            t_m2 = E.op("vector", lambda v: v.tensor_mul(
                srow[:, 32:40], srow[:, 16:24], srow[:, 16:24]), waits=[t_mq], inc="vs")
            vtok = E.op("vector", lambda v: v.tensor_sub(
                srow[:, 32:40], srow[:, 24:32], srow[:, 32:40]), waits=[t_m2], inc="vs")
            sdtok = E.op("scalar", lambda sc: sc.activation(
                srow[:, 40:48], srow[:, 32:40], AF.Sqrt, bias=epsb[:, 0:1]),
                waits=[vtok], inc="ss")
            rctok = E.op("vector", lambda v: v.reciprocal(srow[:, 40:48], srow[:, 40:48]),
                         waits=[sdtok], inc="vs")
            t_nb = E.op("vector", lambda v: v.tensor_mul(
                srow[:, 48:56], srow[:, 16:24], srow[:, 40:48]), waits=[rctok], inc="vs")
            nbtok = E.op("vector", lambda v: v.tensor_scalar(
                srow[:, 48:56], srow[:, 48:56], -1.0, None, OP.mult), waits=[t_nb], inc="vs")
            out_issued = [0]
            for t8 in range(8):
                out_waits = [nbtok]
                if out_issued[0] >= 4:
                    out_waits.append(("d_out", 16 * (out_issued[0] - 3)))
                otok = E.op("vector", lambda v, t8=t8: v.scalar_tensor_tensor(
                    ostage[t8 % 4][:, :], tstage[t8][:, :], srow[:, 40 + t8:41 + t8],
                    bass.AP(srow, 48 + t8, [[64, 128], [0, C]]), OP.mult, OP.add),
                    waits=out_waits, inc="vs")
                E.op("sync", lambda s, t8=t8: s.dma_start(
                    bass.AP(y, t8 * 128 * C, [[C, 128], [1, C]]), ostage[t8 % 4][:, :]),
                    waits=[otok], inc="d_out", inc_n=16)
                out_issued[0] += 1
            E.op("sync", None, waits=[("d_out", 16 * 8)])
            E.flush(nc, sems)

    return nc


_NC_CACHE = []


def kernel(L0_attn_x, L0_attn_w_q, L0_attn_w_k, L0_attn_w_v, L0_attn_w_o,
           L0_ffn_gate_w, L0_ffn_up_w, L0_ffn_down_w,
           L0_ffn_gate_scale, L0_ffn_up_scale, L0_ffn_down_scale):
    x = np.ascontiguousarray(np.asarray(L0_attn_x, dtype=np.float32))
    wq = np.asarray(L0_attn_w_q, np.float32)
    wk = np.asarray(L0_attn_w_k, np.float32)
    wv = np.asarray(L0_attn_w_v, np.float32)
    wo = np.asarray(L0_attn_w_o, np.float32)
    gw = np.ascontiguousarray(np.asarray(L0_ffn_gate_w, np.float32))
    uwv = np.ascontiguousarray(np.asarray(L0_ffn_up_w, np.float32))
    dwv = np.ascontiguousarray(np.asarray(L0_ffn_down_w, np.float32))
    gs = np.asarray(L0_ffn_gate_scale, np.float32)
    us = np.asarray(L0_ffn_up_scale, np.float32)
    ds = np.asarray(L0_ffn_down_scale, np.float32)

    wqT = np.ascontiguousarray(wq.T)
    wkT = np.ascontiguousarray(wk.T)
    wvT = np.ascontiguousarray(wv.T)
    woT = np.ascontiguousarray(wo.T)
    gst = np.ascontiguousarray(gs.reshape(NI, 128).T)
    ust = np.ascontiguousarray(us.reshape(NI, 128).T)
    dstv = np.ascontiguousarray(ds.reshape(CK, 128).T)
    ones = np.full((128, 128), 1.0, np.float32)
    ident = np.eye(128, dtype=np.float32)

    in_maps = []
    for core in range(8):
        b, qc = core // 4, core % 4
        toks, cos, ssn = _core_tables(qc)
        kbmulp, kbms, winm = _core_masks(qc)
        xT = x[b].T
        xTp = np.concatenate([xT, np.zeros((C, 1), np.float32)], axis=1)
        idx = np.where(toks < 0, T, toks)
        xkv = np.ascontiguousarray(xTp[:, idx])
        in_maps.append({
            "xkv": xkv, "wqT": wqT, "wkT": wkT, "wvT": wvT, "woT": woT,
            "gw": gw, "uw": uwv, "dw": dwv,
            "gst": gst, "ust": ust, "dst": dstv,
            "tabc": cos, "tabs": ssn,
            "kbmulp": kbmulp, "kbms": kbms, "winm": winm,
            "ones": ones, "ident": ident,
        })

    if not _NC_CACHE:
        _NC_CACHE.append(build_nc())
    nc = _NC_CACHE[0]
    res = run_bass_kernel_spmd(nc, in_maps, core_ids=list(range(8)))
    out = np.empty((2, T, C), np.float32)
    for core in range(8):
        b, qc = core // 4, core % 4
        out[b, 1024 * qc: 1024 * (qc + 1)] = res.results[core]["y"]
    return out


# version 16
# speedup vs baseline: 1.0172x; 1.0172x over previous
import numpy as np
import ml_dtypes
import concourse.bass as bass
import concourse.mybir as mybir
from concourse.bass_utils import run_bass_kernel_spmd
from contextlib import ExitStack

F32 = mybir.dt.float32
BF16 = mybir.dt.bfloat16
FP8 = mybir.dt.float8e4
AF = mybir.ActivationFunctionType
OP = mybir.AluOpType
AX = mybir.AxisListType

H, D, C, CK, T, I, NI = 16, 128, 2048, 16, 4096, 6144, 48
BLK = 64
NPREF = 10
NZONE = 32
NPACK = NPREF + NZONE
M = NPACK * BLK
OWN = (NPREF + 16) * BLK
NOWN = 1024
GH, NG = 2, 8
SCALE = 1.0 / float(np.sqrt(D))
EPS = 1e-5
NVC = M // 128
TILE_W = [512, 512, 512, 512, 512, 128]
TILE_C0 = [0, 512, 1024, 1536, 2048, 2560]
INV_N = 1.0 / 2048.0


def _allowed(qb, kb):
    return 0 <= kb <= qb and ((qb - kb <= 16) or (kb < 2) or (kb % 4 == 0))


def _core_packed_blocks(qc):
    zone_lo = 16 * qc - 16
    if zone_lo > 0:
        pref = sorted(set([0, 1]) | set(k for k in range(0, zone_lo, 4)))
        pref = [b for b in pref if b < zone_lo]
    else:
        pref = []
    assert len(pref) <= NPREF
    slots = [(pref[s] if s < len(pref) else -1) for s in range(NPREF)]
    slots += [(zone_lo + z if zone_lo + z >= 0 else -1) for z in range(NZONE)]
    return slots, zone_lo


def _seg_table():
    tabs = []
    for itp in range(2):
        segs = []
        for j in range(5):
            segs.append(("pref", 128 * j, 128, j))
        if itp > 0:
            segs.append(("single", (NPREF + 1) * BLK, 64, 0))
            segs.append(("single", (NPREF + 0) * BLK, 64, 1))
            segs.append(("single", (NPREF + 4) * BLK, 64, 2))
        for j in range(12):
            segs.append(("win", (NPREF + 8 * itp + 2 * j) * BLK, 128, (itp, j)))
        tabs.append(segs)
    return tabs


SEGTAB = _seg_table()
NSEG0, NSEG1 = len(SEGTAB[0]), len(SEGTAB[1])


def _core_masks(qc):
    slots, zone_lo = _core_packed_blocks(qc)
    kbmulp = np.zeros((128, 5), np.float32)
    for j in range(5):
        for p in range(128):
            kbmulp[p, j] = 1.0 if slots[2 * j + p // 64] >= 0 else 0.0
    kbms = np.zeros((128, 4), np.float32)
    for ci, z in enumerate([1, 0, 4]):
        raw = zone_lo + z
        ok = 1.0 if (raw >= 0 and (raw < 2 or raw % 4 == 0)) else 0.0
        if ci == 0:
            kbms[64:128, ci] = ok
        else:
            kbms[:64, ci] = ok
    winm = np.zeros((128, 192), np.float32)
    for itp in range(2):
        for j in range(12):
            for p in range(128):
                raw = zone_lo + 8 * itp + 2 * j + p // 64
                for qq in range(8):
                    qb = 16 * qc + 8 * itp + qq
                    winm[p, (itp * 12 + j) * 8 + qq] = float(_allowed(qb, raw))
    cov = np.zeros((64, 70), np.int32)
    for itp in range(2):
        for (kind, col0, npp, ref) in SEGTAB[itp]:
            poff = col0 % 128
            for qq in range(8):
                qb = 16 * qc + 8 * itp + qq
                for sl in range(npp // 64):
                    p = poff + sl * 64
                    raw = slots[(col0 + sl * 64) // 64]
                    if kind == "pref":
                        w = kbmulp[p, ref]
                    elif kind == "single":
                        w = kbms[p, ref]
                    else:
                        w = winm[p, (ref[0] * 12 + ref[1]) * 8 + qq]
                    if w and raw >= 0:
                        cov[qb, raw] += 1
    for qb in range(16 * qc, 16 * qc + 16):
        for kb in range(64):
            want = 1 if _allowed(qb, kb) else 0
            assert cov[qb, kb] == want, (qc, qb, kb, cov[qb, kb], want)
    return kbmulp, kbms, winm


def _core_tables(qc):
    slots, _ = _core_packed_blocks(qc)
    toks = np.full(M, -1, np.int64)
    for s, raw in enumerate(slots):
        if raw >= 0:
            toks[s * 64:(s + 1) * 64] = np.arange(raw * 64, raw * 64 + 64)
    inv = 1.0 / (10000.0 ** (np.arange(0, D, 2, dtype=np.float64) / D))
    pos = np.where(toks < 0, 0, toks).astype(np.float64)
    emb = np.concatenate([np.outer(inv, pos)] * 2, 0)
    cos = np.cos(emb).astype(np.float32)
    sin = np.sin(emb).astype(np.float32)
    cos[:, toks < 0] = 1.0
    sin[:, toks < 0] = 0.0
    ss = sin.copy()
    ss[:64] *= -1.0
    return toks, cos, ss


class Emit:
    ENGINES = ("tensor", "scalar", "vector", "gpsimd", "sync")

    def __init__(self):
        self.events = []
        self.cnt = {}
        self.phase_first = {}
        self.barrier_waits = []

    def begin_phase(self):
        self.barrier_waits = [(s, v) for (s, v) in self.cnt.items() if v > 0]
        self.phase_first = {e: True for e in self.ENGINES}

    def op(self, engine, fn, waits=(), inc=None, inc_n=1):
        waits = [w for w in waits if w is not None]
        if self.phase_first.get(engine):
            waits = self.barrier_waits + waits
            self.phase_first[engine] = False
        incs = []
        tok = None
        if inc is not None:
            self.cnt[inc] = self.cnt.get(inc, 0) + inc_n
            incs.append((inc, inc_n))
            tok = (inc, self.cnt[inc])
        self.events.append((engine, fn, waits, incs))
        return tok

    def flush(self, nc, sems):
        events = self.events
        self.events = []
        with nc.Block() as blk:
            for eng_name in self.ENGINES:
                if not any(e[0] == eng_name for e in events):
                    continue
                def make(eng_name):
                    def body(eng):
                        for (e, fn, waits, incs) in events:
                            if e != eng_name:
                                continue
                            for (s, v) in waits:
                                eng.wait_ge(sems[s], v)
                            if fn is None:
                                continue
                            inst = fn(eng)
                            for (s, n) in incs:
                                inst.then_inc(sems[s], n)
                    return body
                getattr(blk, eng_name)(make(eng_name))


def build_nc():
    nc = bass.Bass("TRN2", target_bir_lowering=False)
    xkv = nc.dram_tensor("xkv", [C, M], F32, kind="ExternalInput")
    wqT = nc.dram_tensor("wqT", [C, C], F32, kind="ExternalInput")
    wkT = nc.dram_tensor("wkT", [C, C], F32, kind="ExternalInput")
    wvT = nc.dram_tensor("wvT", [C, C], F32, kind="ExternalInput")
    woT = nc.dram_tensor("woT", [C, C], F32, kind="ExternalInput")
    gw = nc.dram_tensor("gw", [C, I], F32, kind="ExternalInput")
    uw = nc.dram_tensor("uw", [C, I], F32, kind="ExternalInput")
    dw = nc.dram_tensor("dw", [I, C], F32, kind="ExternalInput")
    gst_d = nc.dram_tensor("gst", [128, NI], F32, kind="ExternalInput")
    ust_d = nc.dram_tensor("ust", [128, NI], F32, kind="ExternalInput")
    dst_d = nc.dram_tensor("dst", [128, CK], F32, kind="ExternalInput")
    tabc_d = nc.dram_tensor("tabc", [128, M], F32, kind="ExternalInput")
    tabs_d = nc.dram_tensor("tabs", [128, M], F32, kind="ExternalInput")
    kbmulp_d = nc.dram_tensor("kbmulp", [128, 5], F32, kind="ExternalInput")
    kbms_d = nc.dram_tensor("kbms", [128, 4], F32, kind="ExternalInput")
    winm_d = nc.dram_tensor("winm", [128, 192], F32, kind="ExternalInput")
    ones_d = nc.dram_tensor("ones", [128, 128], F32, kind="ExternalInput")
    ident_d = nc.dram_tensor("ident", [128, 128], F32, kind="ExternalInput")
    y = nc.dram_tensor("y", [NOWN, C], F32, kind="ExternalOutput")

    xkv_r = xkv.ap().rearrange("(ck p) t -> p ck t", p=128)
    wqT_r = wqT.ap().rearrange("(ck p) j -> p ck j", p=128)
    wkT_r = wkT.ap().rearrange("(ck p) j -> p ck j", p=128)
    wvT_r = wvT.ap().rearrange("(ck p) j -> p ck j", p=128)
    woT_r = woT.ap().rearrange("(ck p) j -> p ck j", p=128)
    gw_r = gw.ap().rearrange("(ck p) i -> p ck i", p=128)
    uw_r = uw.ap().rearrange("(ck p) i -> p ck i", p=128)
    dw_r = dw.ap().rearrange("(ik p) h -> p ik h", p=128)

    E = Emit()
    sem_names = ["ts", "ss", "vs", "gs", "d_xb0", "d_xb1", "d_w", "d_wo0", "d_wo1",
                 "d_gu0", "d_gu1", "d_ds0", "d_ds1", "d_misc", "d_out"]

    with ExitStack() as outer:
        ent = outer.enter_context
        actb = ent(nc.sbuf_tensor("actb", [128, H * 1024], BF16))
        gst = ent(nc.sbuf_tensor("gsts", [128, NI], F32))
        ust = ent(nc.sbuf_tensor("usts", [128, NI], F32))
        dst = ent(nc.sbuf_tensor("dsts", [128, CK], F32))
        kbmulp = ent(nc.sbuf_tensor("kbmulps", [128, 5], F32))
        kbms = ent(nc.sbuf_tensor("kbmss", [128, 4], F32))
        winm = ent(nc.sbuf_tensor("winms", [128, 192], F32))
        onesf = ent(nc.sbuf_tensor("onesf", [128, 128], F32))
        epsb = ent(nc.sbuf_tensor("epsb", [128, 1], F32))
        identb = ent(nc.sbuf_tensor("identb", [128, 128], BF16))
        sems = {n: ent(nc.semaphore(n)) for n in sem_names}
        wob = [ent(nc.sbuf_tensor("wob0", [128, CK * 256], BF16)), None]
        wo_last_use = [None, None]
        wo_loads = [0]
        def load_wo(band):
            E.op("gpsimd", lambda gp, band=band: gp.dma_start(
                wob[band % 2][:, :].rearrange("p (ck j) -> p ck j", j=256),
                woT_r[:, :, band * 256:(band + 1) * 256]),
                waits=[wo_last_use[band % 2]], inc=f"d_wo{band % 2}", inc_n=16)
            wo_loads[0] += 1

        with ExitStack() as ph:
            pe = ph.enter_context
            xbt = [pe(nc.sbuf_tensor(f"xbt{i}", [128, CK * 512], BF16)) for i in range(2)]
            wqg = pe(nc.sbuf_tensor("wqg", [128, CK * 256], BF16))
            wkg = pe(nc.sbuf_tensor("wkg", [128, CK * 256], BF16))
            wvg = pe(nc.sbuf_tensor("wvg", [128, CK * 256], BF16))
            kbg = pe(nc.sbuf_tensor("kbg", [128, GH * M], BF16))
            qbg = pe(nc.sbuf_tensor("qbg", [128, GH * 1024], BF16))
            vg = pe(nc.sbuf_tensor("vg", [128, NVC * 256], BF16))
            tabc = pe(nc.sbuf_tensor("tabc_s", [128, M], F32))
            tabs = pe(nc.sbuf_tensor("tabs_s", [128, M], F32))
            probs = [pe(nc.sbuf_tensor(f"probs{i}", [128, 1024], BF16)) for i in range(3)]
            probsm = [pe(nc.sbuf_tensor(f"probsm{i}", [128, 1024], BF16)) for i in range(3)]
            onesb = pe(nc.sbuf_tensor("onesb", [128, 128], BF16))
            winmfull = pe(nc.sbuf_tensor("winmfull", [128, 24 * 512], BF16))
            preffull = pe(nc.sbuf_tensor("preffull", [128, 5 * 512], BF16))
            snglfull = pe(nc.sbuf_tensor("snglfull", [128, 3 * 512], BF16))
            recipb = pe(nc.sbuf_tensor("recipb", [128, 1024], F32))
            sums_sb = pe(nc.sbuf_tensor("sums_sb", [128, 2 * 1024], F32))
            rt1 = pe(nc.sbuf_tensor("rt1", [128, 512], F32))
            rt2 = pe(nc.sbuf_tensor("rt2", [128, 512], F32))
            pp = [pe(nc.psum_tensor(f"pp{i}", [128, 512], F32)) for i in range(2)]
            sps = [pe(nc.psum_tensor(f"sps{i}", [128, 1024], F32)) for i in range(2)]
            cps = [pe(nc.psum_tensor(f"cps{i}", [128, 512], F32)) for i in range(2)]

            E.begin_phase()
            for (dstt, src) in [(tabc, tabc_d), (tabs, tabs_d), (kbmulp, kbmulp_d),
                                (kbms, kbms_d), (winm, winm_d), (onesf, ones_d),
                                (gst, gst_d), (ust, ust_d), (dst, dst_d)]:
                E.op("sync", lambda s, d=dstt, so=src: s.dma_start(d[:, :], so.ap()),
                     inc="d_misc", inc_n=16)
            E.op("gpsimd", lambda g: g.dma_start(identb[:, :], ident_d.ap()),
                 inc="d_misc", inc_n=16)
            t_misc = ("d_misc", E.cnt["d_misc"])
            E.op("vector", lambda v: v.memset(epsb[:, :], EPS))
            for itp in range(2):
                for j in range(12):
                    moff = (itp * 12 + j) * 8
                    E.op("vector", lambda v, itp=itp, j=j, moff=moff: v.tensor_tensor(
                        winmfull[:, (itp * 12 + j) * 512:(itp * 12 + j + 1) * 512],
                        bass.AP(winm, moff, [[192, 128], [1, 8], [0, 64]]),
                        bass.AP(winm, moff, [[192, 128], [1, 8], [0, 64]]), OP.bypass),
                        waits=[t_misc] if (itp == 0 and j == 0) else ())
            for j in range(5):
                E.op("vector", lambda v, j=j: v.tensor_tensor(
                    preffull[:, 512 * j: 512 * (j + 1)],
                    bass.AP(kbmulp, j, [[5, 128], [0, 512]]),
                    bass.AP(kbmulp, j, [[5, 128], [0, 512]]), OP.bypass))
            for j in range(3):
                E.op("vector", lambda v, j=j: v.tensor_tensor(
                    snglfull[:, 512 * j: 512 * (j + 1)],
                    bass.AP(kbms, j, [[4, 128], [0, 512]]),
                    bass.AP(kbms, j, [[4, 128], [0, 512]]), OP.bypass))
            t_masks = E.op("vector", lambda v: v.engine_nop(), inc="vs")
            t_onesb = E.op("scalar", lambda sc: sc.copy(onesb[:, :], onesf[:, :]),
                           waits=[t_misc], inc="ss")
            sps_free = [None, None]
            sps_free[0] = E.op("vector", lambda v: v.memset(sps[0][:, :], 0.0),
                               waits=[], inc="vs")
            sps_free[1] = E.op("vector", lambda v: v.memset(sps[1][:, :], 0.0), inc="vs")

            xb_free = [None, None]
            xb_refills = [0, 0]
            xb_prefetched = {}
            xb_tok = [None, None]
            pp_free = [None, None]
            probs_free = [None, None, None]
            pm_free = [None, None, None]
            cps_free = [None, None]
            w_tok = [None]
            pcnt = [0]
            expb = [0]

            def group_weight_load(g, after_tok):
                waits = [after_tok]
                for (dstt, src) in [(wqg, wqT_r), (wkg, wkT_r), (wvg, wvT_r)]:
                    E.op("gpsimd",
                         lambda gp, d=dstt, so=src, gg=g: gp.dma_start(
                             d[:, :].rearrange("p (ck j) -> p ck j", j=256),
                             so[:, :, gg * 256:(gg + 1) * 256]),
                         waits=waits, inc="d_w", inc_n=16)
                    waits = []
                w_tok[0] = ("d_w", E.cnt["d_w"])

            def proj(lhs_fn, rhs_fn, out_w, first_waits):
                ps = pcnt[0] % 2
                pcnt[0] += 1
                waits = list(first_waits) + ([pp_free[ps]] if pp_free[ps] else [])
                tok = None
                for ci in range(CK):
                    tok = E.op("tensor",
                               lambda te, ci=ci, ps=ps, ow=out_w: te.matmul(
                                   pp[ps][:, 0:ow], lhs_fn(ci), rhs_fn(ci),
                                   start=(ci == 0), stop=(ci == CK - 1)),
                               waits=waits if ci == 0 else (),
                               inc="ts" if ci == CK - 1 else None)
                    waits = []
                return ps, tok

            def rope_evac(ps, mmtok, dst_t, dcol, src_c0, w):
                E.op("vector", lambda v: v.tensor_mul(
                    rt1[:, 0:w], pp[ps][:, 0:w], tabc[:, src_c0:src_c0 + w]),
                    waits=[mmtok, t_misc])
                E.op("vector", lambda v: v.tensor_mul(
                    rt2[0:64, 0:w], pp[ps][64:128, 0:w], tabs[0:64, src_c0:src_c0 + w]))
                E.op("vector", lambda v: v.tensor_mul(
                    rt2[64:128, 0:w], pp[ps][0:64, 0:w], tabs[64:128, src_c0:src_c0 + w]))
                return E.op("vector", lambda v: v.tensor_add(
                    dst_t[:, dcol:dcol + w], rt1[:, 0:w], rt2[:, 0:w]), inc="vs")

            group_weight_load(0, None)
            for g in range(NG):
                g_w_tok = w_tok[0]
                rope_done_h = [None, None]
                q_done_h = [None, None]
                vproj_done = None
                for t in range(6):
                    w_t, c0 = TILE_W[t], TILE_C0[t]
                    st = t % 2
                    if (g, t) in xb_prefetched:
                        xb_tok[st] = xb_prefetched.pop((g, t))
                    else:
                        E.op("gpsimd",
                             lambda gp, st=st, c0=c0, w_t=w_t: gp.dma_start(
                                 xbt[st][:, :].rearrange("p (ck t) -> p ck t", t=512)[:, :, 0:w_t],
                                 xkv_r[:, :, c0:c0 + w_t]),
                             waits=[xb_free[st]], inc=f"d_xb{st}", inc_n=16)
                        xb_refills[st] += 1
                        xb_tok[st] = (f"d_xb{st}", 16 * xb_refills[st])
                    for l in range(GH):
                        ps, tok = proj(
                            lambda ci, l=l: wkg[:, 256 * ci + 128 * l: 256 * ci + 128 * (l + 1)],
                            lambda ci, st=st, w_t=w_t: xbt[st][:, 512 * ci: 512 * ci + w_t],
                            w_t, [xb_tok[st], g_w_tok])
                        rope_done_h[l] = rope_evac(ps, tok, kbg, l * M + c0, c0, w_t)
                        pp_free[ps] = rope_done_h[l]
                    q_lo, q_hi = max(c0, OWN), c0 + w_t
                    if q_hi > q_lo:
                        off, qw = q_lo - c0, q_hi - q_lo
                        for l in range(GH):
                            ps, tok = proj(
                                lambda ci, l=l: wqg[:, 256 * ci + 128 * l: 256 * ci + 128 * (l + 1)],
                                lambda ci, st=st, off=off, qw=qw: xbt[st][:, 512 * ci + off: 512 * ci + off + qw],
                                qw, [xb_tok[st], g_w_tok])
                            q_done_h[l] = rope_evac(ps, tok, qbg, l * 1024 + (q_lo - OWN), q_lo, qw)
                            pp_free[ps] = q_done_h[l]
                    for s in range(w_t // 128):
                        r = c0 // 128 + s
                        ps, tok = proj(
                            lambda ci, st=st, s=s: xbt[st][:, 512 * ci + 128 * s: 512 * ci + 128 * (s + 1)],
                            lambda ci: wvg[:, 256 * ci: 256 * (ci + 1)],
                            256, [xb_tok[st], g_w_tok])
                        vproj_done = E.op("scalar", lambda sc, ps=ps, r=r: sc.copy(
                            vg[:, 256 * r: 256 * (r + 1)], pp[ps][:, 0:256]),
                            waits=[tok], inc="ss")
                        pp_free[ps] = vproj_done
                    xb_free[st] = ("ts", E.cnt["ts"])
                if g + 1 < NG:
                    group_weight_load(g + 1, ("ts", E.cnt["ts"]))
                    for t in range(2):
                        st = t % 2
                        E.op("gpsimd",
                             lambda gp, st=st, c0=TILE_C0[t], w_t=TILE_W[t]: gp.dma_start(
                                 xbt[st][:, :].rearrange("p (ck t) -> p ck t", t=512)[:, :, 0:w_t],
                                 xkv_r[:, :, c0:c0 + w_t]),
                             waits=[xb_free[st]], inc=f"d_xb{st}", inc_n=16)
                        xb_refills[st] += 1
                        xb_prefetched[(g + 1, t)] = (f"d_xb{st}", 16 * xb_refills[st])

                head_ev = {}
                for l in range(GH):
                    h = GH * g + l
                    hs = h % 2
                    sums_last = None
                    batches = []
                    for itp in range(2):
                        segs = SEGTAB[itp]
                        for bi, b0 in enumerate(range(0, len(segs), 2)):
                            batches.append((itp, bi, segs[b0:b0 + 2], b0 + 2 >= len(segs)))
                    pend_q = []
                    for bk in range(len(batches) + 2):
                        if bk < len(batches):
                            (itp, bi, segs_b, last_of_it) = batches[bk]
                            sp = expb[0] % 2
                            expb[0] += 1
                            pslot = expb[0] % 3
                            sc_waits = [sps_free[sp]]
                            if bk == 0:
                                sc_waits += [rope_done_h[l], q_done_h[l], t_masks]
                            last_mm = None
                            for si, (kind, col0, npp, ref) in enumerate(segs_b):
                                poff = col0 % 128
                                last_mm = E.op("tensor",
                                    lambda te, si=si, col0=col0, npp=npp, poff=poff, l=l, itp=itp, sp=sp: te.matmul(
                                        sps[sp][poff:poff + npp, 512 * si: 512 * si + 512],
                                        kbg[:, l * M + col0: l * M + col0 + npp],
                                        qbg[:, l * 1024 + itp * 512: l * 1024 + (itp + 1) * 512],
                                        start=True, stop=True),
                                    waits=sc_waits if si == 0 else (), inc="ts")
                                sc_waits = []
                            fd = 512 * len(segs_b)
                            exp_waits = [w for w in (last_mm, probs_free[pslot]) if w]
                            etok = E.op("scalar",
                                lambda sc, fd=fd, sp=sp, pslot=pslot: sc.activation(
                                    probs[pslot][:, 0:fd], sps[sp][:, 0:fd],
                                    AF.Exp, scale=SCALE),
                                waits=exp_waits, inc="ss")
                            sps_free[sp] = etok
                            pend_q.append((itp, bi, segs_b, last_of_it, pslot, etok))
                        if (len(pend_q) > 2) or (bk >= len(batches) and pend_q):
                            (pit, pbi, psegs, plast, pslot, petok) = pend_q.pop(0)
                            cs = (2 * h + pit) % 2
                            ssl = (2 * h + pit) % 2
                            for si, (kind, col0, npp, ref) in enumerate(psegs):
                                pcol = 512 * si
                                poff = col0 % 128
                                if kind == "pref":
                                    msrc, mo = preffull, 512 * ref
                                elif kind == "single":
                                    msrc, mo = snglfull, 512 * ref
                                else:
                                    msrc, mo = winmfull, 512 * (ref[0] * 12 + ref[1])
                                mw = [petok, pm_free[pslot]] if si == 0 else [petok]
                                mtok = E.op("vector", lambda v, pcol=pcol, msrc=msrc, mo=mo, pslot=pslot: v.tensor_tensor(
                                    probsm[pslot][:, pcol:pcol + 512], probs[pslot][:, pcol:pcol + 512],
                                    msrc[:, mo:mo + 512], OP.mult),
                                    waits=[w for w in mw if w], inc="vs")
                                cw = [mtok]
                                first_ctx = (pbi == 0 and si == 0)
                                if first_ctx:
                                    cw.append(vproj_done)
                                    if cps_free[cs] is not None:
                                        cw.append(cps_free[cs])
                                r0 = col0 // 128
                                is_last = plast and (si == len(psegs) - 1)
                                ctx_tok = E.op("tensor",
                                    lambda te, r0=r0, poff=poff, npp=npp, l=l, pcol=pcol, cs=cs,
                                           fc=first_ctx, st_=is_last, pslot=pslot: te.matmul(
                                        cps[cs][:, 0:512],
                                        vg[poff:poff + npp, 256 * r0 + 128 * l: 256 * r0 + 128 * (l + 1)],
                                        probsm[pslot][poff:poff + npp, pcol:pcol + 512],
                                        start=fc, stop=st_),
                                    waits=cw, inc="ts")
                                smw = [t_onesb] if (first_ctx and h == 0) else []
                                if first_ctx and pp_free[ssl] is not None:
                                    smw.append(pp_free[ssl])
                                sum_tok = E.op("tensor",
                                    lambda te, poff=poff, npp=npp, pcol=pcol, ssl=ssl,
                                           fc=first_ctx, st_=is_last, pslot=pslot: te.matmul(
                                        pp[ssl][:, 0:512],
                                        onesb[poff:poff + npp, 0:128],
                                        probsm[pslot][poff:poff + npp, pcol:pcol + 512],
                                        start=fc, stop=st_),
                                    waits=smw, inc="ts")
                            probs_free[pslot] = ("vs", E.cnt["vs"])
                            pm_free[pslot] = ("ts", E.cnt["ts"])
                            if plast:
                                ev = E.op("vector", lambda v, h=h, it=pit, cs=cs: v.tensor_copy(
                                    actb[:, 1024 * h + 512 * it: 1024 * h + 512 * (it + 1)],
                                    cps[cs][:, 0:512]), waits=[ctx_tok], inc="vs")
                                cps_free[cs] = ev
                                stk = E.op("vector", lambda v, l=l, it=pit, ssl=ssl: v.tensor_copy(
                                    sums_sb[:, 1024 * l + 512 * it: 1024 * l + 512 * (it + 1)],
                                    pp[ssl][:, 0:512]),
                                    waits=[sum_tok], inc="vs")
                                pp_free[ssl] = stk
                                head_ev[(l, pit)] = ev
                for l in range(GH):
                    h = GH * g + l
                    for itp in range(2):
                        E.op("vector", lambda v, l=l, itp=itp: v.reciprocal(
                            recipb[:, 512 * itp: 512 * (itp + 1)],
                            sums_sb[:, 1024 * l + 512 * itp: 1024 * l + 512 * (itp + 1)]))
                    E.op("vector", lambda v, h=h: v.tensor_mul(
                        actb[:, 1024 * h: 1024 * (h + 1)],
                        actb[:, 1024 * h: 1024 * (h + 1)], recipb[:, :]),
                        waits=[head_ev[(l, 0)], head_ev[(l, 1)]], inc="vs")
            load_wo(0)
            E.flush(nc, sems)

        phBC = outer.enter_context(ExitStack())
        peBC = phBC.enter_context
        wob[1] = peBC(nc.sbuf_tensor("wob1", [128, CK * 256], BF16))
        gsl = [peBC(nc.sbuf_tensor(f"gsl{i}", [128, CK * 256], BF16)) for i in range(2)]
        usl = [peBC(nc.sbuf_tensor(f"usl{i}", [128, CK * 256], BF16)) for i in range(2)]
        g8 = [peBC(nc.sbuf_tensor(f"g8_{i}", [128, CK * 256], FP8)) for i in range(2)]
        u8 = [peBC(nc.sbuf_tensor(f"u8_{i}", [128, CK * 256], FP8)) for i in range(2)]
        gu_last_use = [None, None]
        def load_gu(band):
            slot = band % 2
            sem = f"d_gu{slot}"
            E.op("gpsimd", lambda gp, band=band, slot=slot: gp.dma_start(
                g8[slot][:, :].rearrange("p (ck i) -> p ck i", i=256),
                gw_r[:, :, band * 256:(band + 1) * 256]),
                waits=[gu_last_use[slot]], inc=sem, inc_n=16)
            t_u8 = E.op("gpsimd", lambda gp, band=band, slot=slot: gp.dma_start(
                u8[slot][:, :].rearrange("p (ck i) -> p ck i", i=256),
                uw_r[:, :, band * 256:(band + 1) * 256]), inc=sem, inc_n=16)
            E.op("gpsimd", lambda gp, slot=slot: gp.dma_start(
                gsl[slot][:, :], g8[slot][:, :]), waits=[t_u8], inc=sem, inc_n=16)
            E.op("gpsimd", lambda gp, slot=slot: gp.dma_start(
                usl[slot][:, :], u8[slot][:, :]), inc=sem, inc_n=16)
            return (sem, E.cnt[sem])

        with ExitStack() as ph:
            pe = ph.enter_context
            xrest = pe(nc.sbuf_tensor("xrest", [128, CK * 1024], F32))
            sqt = [pe(nc.sbuf_tensor(f"sqt{i}", [128, 1024], F32)) for i in range(2)]
            meanb = pe(nc.sbuf_tensor("meanb", [128, 1024], F32))
            varb = pe(nc.sbuf_tensor("varb", [128, 1024], F32))
            rstdb = pe(nc.sbuf_tensor("rstdb", [128, 1024], F32))
            tmpn = pe(nc.sbuf_tensor("tmpn", [128, 1024], F32))
            ppB = [pe(nc.psum_tensor(f"ppB{i}", [128, 512], F32)) for i in range(3)]
            st1 = pe(nc.psum_tensor("st1", [128, 1024], F32))
            st2 = pe(nc.psum_tensor("st2", [128, 1024], F32))

            E.begin_phase()
            load_wo(1)
            gu_tok = [None, None]
            gu_tok[0] = load_gu(0)
            gu_tok[1] = load_gu(1)
            E.op("sync", lambda s: s.dma_start(
                xrest[:, :].rearrange("p (ck t) -> p ck t", t=1024),
                xkv_r[:, :, OWN:OWN + 1024]), inc="d_misc", inc_n=16)
            t_xres = ("d_misc", E.cnt["d_misc"])

            ppB_free = [None, None, None]
            pcntB = 0
            sq_used = [None, None]
            s1_tok = s2_tok = None
            for band in range(8):
                slot = band % 2
                band_dma_target = (f"d_wo{band % 2}", 16 * (band // 2 + 1))
                for j4 in range(2):
                    j = 2 * band + j4
                    ev_j = []
                    for tt in range(2):
                        ps = pcntB % 3
                        pcntB += 1
                        waits = [band_dma_target, ppB_free[ps]]
                        mmtok = None
                        for ci in range(CK):
                            mmtok = E.op("tensor", lambda te, ci=ci, j4=j4, tt=tt, ps=ps, slot=slot: te.matmul(
                                ppB[ps][:, 0:512],
                                wob[slot][:, 256 * ci + 128 * j4: 256 * ci + 128 * (j4 + 1)],
                                actb[:, 1024 * ci + 512 * tt: 1024 * ci + 512 * (tt + 1)],
                                start=(ci == 0), stop=(ci == CK - 1)),
                                waits=waits if ci == 0 else (), inc="ts" if ci == CK - 1 else None)
                            waits = []
                        wo_last_use[slot] = mmtok
                        ev = E.op("vector", lambda v, j=j, tt=tt, ps=ps: v.tensor_add(
                            xrest[:, 1024 * j + 512 * tt: 1024 * j + 512 * (tt + 1)],
                            ppB[ps][:, 0:512],
                            xrest[:, 1024 * j + 512 * tt: 1024 * j + 512 * (tt + 1)]),
                            waits=[mmtok, t_xres], inc="vs")
                        ppB_free[ps] = ev
                        ev_j.append(ev)
                    ci = j
                    sl = ci % 2
                    sqtok = E.op("scalar", lambda sc, ci=ci, sl=sl: sc.activation(
                        sqt[sl][:, :], xrest[:, 1024 * ci: 1024 * (ci + 1)], AF.Square),
                        waits=[ev_j[0], ev_j[1], sq_used[sl]], inc="ss")
                    for half in range(2):
                        s1_tok = E.op("tensor", lambda te, ci=ci, half=half: te.matmul(
                            st1[:, 512 * half: 512 * (half + 1)], onesf[:, :],
                            xrest[:, 1024 * ci + 512 * half: 1024 * ci + 512 * (half + 1)],
                            start=(ci == 0), stop=(ci == CK - 1)),
                            waits=[ev_j[half]], inc="ts")
                    for half in range(2):
                        s2_tok = E.op("tensor", lambda te, ci=ci, half=half, sl=sl: te.matmul(
                            st2[:, 512 * half: 512 * (half + 1)], onesf[:, :],
                            sqt[sl][:, 512 * half: 512 * (half + 1)],
                            start=(ci == 0), stop=(ci == CK - 1)),
                            waits=[sqtok] if half == 0 else (), inc="ts")
                    sq_used[sl] = ("ts", E.cnt["ts"])
                if band + 2 < 8:
                    load_wo(band + 2)
            E.op("vector", lambda v: v.tensor_scalar(
                meanb[:, :], st1[:, :], INV_N, None, OP.mult), waits=[s1_tok])
            E.op("vector", lambda v: v.tensor_scalar(
                varb[:, :], st2[:, :], INV_N, None, OP.mult), waits=[s2_tok])
            E.op("vector", lambda v: v.tensor_mul(tmpn[:, :], meanb[:, :], meanb[:, :]))
            vtok = E.op("vector", lambda v: v.tensor_sub(varb[:, :], varb[:, :], tmpn[:, :]),
                        inc="vs")
            sdtok = E.op("scalar", lambda sc: sc.activation(
                rstdb[:, :], varb[:, :], AF.Sqrt, bias=epsb[:, 0:1]), waits=[vtok], inc="ss")
            E.op("vector", lambda v: v.reciprocal(rstdb[:, :], rstdb[:, :]), waits=[sdtok])
            for ci in range(CK):
                E.op("vector", lambda v, ci=ci: v.tensor_sub(
                    tmpn[:, :], xrest[:, 1024 * ci: 1024 * (ci + 1)], meanb[:, :]))
                E.op("vector", lambda v, ci=ci: v.tensor_mul(
                    actb[:, 1024 * ci: 1024 * (ci + 1)], tmpn[:, :], rstdb[:, :]),
                    inc="vs")
            E.flush(nc, sems)

        with ExitStack() as ph:
            pe = ph.enter_context
            fuse = pe(nc.sbuf_tensor("fuse", [128, NI * 1024], BF16))
            phC = ph.enter_context(ExitStack())
            peC = phC.enter_context
            gtile = [peC(nc.sbuf_tensor(f"gtile{i}", [128, 1024], BF16)) for i in range(2)]
            utile = [peC(nc.sbuf_tensor(f"utile{i}", [128, 1024], BF16)) for i in range(2)]
            pg = [peC(nc.psum_tensor(f"pg{i}", [128, 512], F32)) for i in range(2)]
            pu = [peC(nc.psum_tensor(f"pu{i}", [128, 512], F32)) for i in range(2)]

            E.begin_phase()
            pg_free = [None, None]
            pu_free = [None, None]
            gt_free = [None, None]
            for band in range(24):
                slot = band % 2
                for i4 in range(2):
                    i = 2 * band + i4
                    isl = i % 2
                    gdone = udone = None
                    for tt in range(2):
                        ps = (2 * i + tt) % 2
                        waits = [gu_tok[slot], pg_free[ps]]
                        mmtok = None
                        for ci in range(CK):
                            mmtok = E.op("tensor", lambda te, ci=ci, i4=i4, tt=tt, ps=ps, slot=slot: te.matmul(
                                pg[ps][:, 0:512],
                                gsl[slot][:, 256 * ci + 128 * i4: 256 * ci + 128 * (i4 + 1)],
                                actb[:, 1024 * ci + 512 * tt: 1024 * ci + 512 * (tt + 1)],
                                start=(ci == 0), stop=(ci == CK - 1)),
                                waits=waits if ci == 0 else (), inc="ts" if ci == CK - 1 else None)
                            waits = []
                        gdone = E.op("scalar", lambda sc, i=i, tt=tt, ps=ps, isl=isl: sc.activation(
                            gtile[isl][:, 512 * tt: 512 * (tt + 1)], pg[ps][:, 0:512],
                            AF.Silu, scale=gst[:, i:i + 1]),
                            waits=[mmtok, gt_free[isl] if tt == 0 else None], inc="ss")
                        pg_free[ps] = gdone
                    for tt in range(2):
                        ps = (2 * i + tt) % 2
                        waits = [gu_tok[slot], pu_free[ps]]
                        mmtok = None
                        for ci in range(CK):
                            mmtok = E.op("tensor", lambda te, ci=ci, i4=i4, tt=tt, ps=ps, slot=slot: te.matmul(
                                pu[ps][:, 0:512],
                                usl[slot][:, 256 * ci + 128 * i4: 256 * ci + 128 * (i4 + 1)],
                                actb[:, 1024 * ci + 512 * tt: 1024 * ci + 512 * (tt + 1)],
                                start=(ci == 0), stop=(ci == CK - 1)),
                                waits=waits if ci == 0 else (), inc="ts" if ci == CK - 1 else None)
                            waits = []
                        udone = E.op("scalar", lambda sc, i=i, tt=tt, ps=ps, isl=isl: sc.activation(
                            utile[isl][:, 512 * tt: 512 * (tt + 1)], pu[ps][:, 0:512],
                            AF.Copy, scale=ust[:, i:i + 1]),
                            waits=[mmtok, gt_free[isl] if tt == 0 else None], inc="ss")
                        pu_free[ps] = udone
                    gt_free[isl] = E.op("vector", lambda v, i=i, isl=isl: v.tensor_mul(
                        fuse[:, 1024 * i: 1024 * (i + 1)], gtile[isl][:, :], utile[isl][:, :]),
                        waits=[gdone, udone], inc="vs")
                gu_last_use[slot] = ("ts", E.cnt["ts"])
                if band + 2 < 24:
                    gu_tok[slot] = load_gu(band + 2)
            E.flush(nc, sems)
            phC.close()

            with ExitStack() as ph2:
                pe2 = ph2.enter_context
                dsl = [pe2(nc.sbuf_tensor(f"dsl{i}", [128, 4 * 512], BF16)) for i in range(2)]
                d8 = [pe2(nc.sbuf_tensor(f"d8_{i}", [128, 4 * 512], FP8)) for i in range(2)]
                pd = [pe2(nc.psum_tensor(f"pd{i}", [128, 512], F32)) for i in range(8)]

                E.begin_phase()
                ds_last_use = [None, None]
                def load_ds(band, ig):
                    slot = (band * 12 + ig) % 2
                    sem = f"d_ds{slot}"
                    t_d8 = E.op("gpsimd", lambda gp, band=band, ig=ig, slot=slot: gp.dma_start(
                        d8[slot][:, :].rearrange("p (ik h) -> p ik h", h=512),
                        dw_r[:, 4 * ig: 4 * ig + 4, 512 * band: 512 * (band + 1)]),
                        waits=[ds_last_use[slot]], inc=sem, inc_n=16)
                    E.op("gpsimd", lambda gp, slot=slot: gp.dma_start(
                        dsl[slot][:, :], d8[slot][:, :]), waits=[t_d8], inc=sem, inc_n=16)
                    return (sem, E.cnt[sem])
                pd_free = [None] * 8
                for band in range(4):
                    ds_tok = {}
                    ds_tok[0] = load_ds(band, 0)
                    ds_tok[1] = load_ds(band, 1)
                    for ig in range(12):
                        slot = (band * 12 + ig) % 2
                        for ii in range(4):
                            i = 4 * ig + ii
                            for hh in range(4):
                                h = 4 * band + hh
                                for tt in range(2):
                                    ps = 2 * hh + tt
                                    waits = []
                                    if i == 0:
                                        waits = [ds_tok[ig], pd_free[ps]]
                                    elif ii == 0:
                                        waits = [ds_tok[ig]]
                                    mmtok = E.op("tensor",
                                        lambda te, ii=ii, hh=hh, tt=tt, ps=ps, slot=slot, i=i: te.matmul(
                                            pd[ps][:, 0:512],
                                            dsl[slot][:, 512 * ii + 128 * hh: 512 * ii + 128 * (hh + 1)],
                                            fuse[:, 1024 * i + 512 * tt: 1024 * i + 512 * (tt + 1)],
                                            start=(i == 0), stop=(i == NI - 1)),
                                        waits=waits,
                                        inc="ts" if (i == NI - 1 or (ii == 3)) else None)
                        ds_last_use[slot] = ("ts", E.cnt["ts"])
                        if ig + 2 < 12:
                            ds_tok[ig + 2] = load_ds(band, ig + 2)
                    last_tok = ("ts", E.cnt["ts"])
                    for hh in range(4):
                        h = 4 * band + hh
                        for tt in range(2):
                            ps = 2 * hh + tt
                            pd_free[ps] = E.op("vector",
                                lambda v, h=h, tt=tt, ps=ps: v.scalar_tensor_tensor(
                                    actb[:, 1024 * h + 512 * tt: 1024 * h + 512 * (tt + 1)],
                                    pd[ps][:, 0:512], dst[:, h:h + 1],
                                    actb[:, 1024 * h + 512 * tt: 1024 * h + 512 * (tt + 1)],
                                    OP.mult, OP.add),
                                waits=[last_tok], inc="vs")
                E.flush(nc, sems)
        phBC.close()

        with ExitStack() as ph:
            pe = ph.enter_context
            tstage = [pe(nc.sbuf_tensor(f"tstage{i}", [128, C], F32)) for i in range(8)]
            ostage = [pe(nc.sbuf_tensor(f"ostage{i}", [128, C], F32)) for i in range(4)]
            trash = [pe(nc.sbuf_tensor(f"trash{i}", [128, C], F32)) for i in range(2)]
            srow = pe(nc.sbuf_tensor("srow", [128, 64], F32))
            ptr = [pe(nc.psum_tensor(f"ptr{i}", [128, 1024], BF16)) for i in range(4)]

            E.begin_phase()
            ptr_free = [None, None, None, None]
            sq_free = [None, None]
            qtok = None
            for t8 in range(8):
                cp_tok = None
                for ci in range(CK):
                    ps = ci % 4
                    trtok = E.op("tensor", lambda te, ci=ci, t8=t8, ps=ps: te.transpose(
                        ptr[ps][:, 0:128], actb[:, 1024 * ci + 128 * t8: 1024 * ci + 128 * (t8 + 1)],
                        identb[:, :]),
                        waits=[ptr_free[ps], t_misc if (t8 == 0 and ci == 0) else None],
                        inc="ts")
                    cp_tok = E.op("vector", lambda v, ci=ci, t8=t8, ps=ps: v.tensor_copy(
                        tstage[t8][:, 128 * ci: 128 * (ci + 1)], ptr[ps][:, 0:128]),
                        waits=[trtok], inc="vs")
                    ptr_free[ps] = cp_tok
                E.op("vector", lambda v, t8=t8: v.reduce_sum(
                    srow[:, t8:t8 + 1], tstage[t8][:, :], AX.X), waits=[cp_tok])
                qtok = E.op("scalar", lambda sc, t8=t8: sc.activation(
                    trash[t8 % 2][:, :], tstage[t8][:, :], AF.Square,
                    accum_out=srow[:, 8 + t8:9 + t8]),
                    waits=[cp_tok, sq_free[t8 % 2]], inc="ss")
                sq_free[t8 % 2] = qtok
            t_m = E.op("vector", lambda v: v.tensor_scalar(
                srow[:, 16:24], srow[:, 0:8], INV_N, None, OP.mult),
                waits=[qtok, sq_free[0], sq_free[1]], inc="vs")
            t_mq = E.op("vector", lambda v: v.tensor_scalar(
                srow[:, 24:32], srow[:, 8:16], INV_N, None, OP.mult), waits=[t_m], inc="vs")
            t_m2 = E.op("vector", lambda v: v.tensor_mul(
                srow[:, 32:40], srow[:, 16:24], srow[:, 16:24]), waits=[t_mq], inc="vs")
            vtok = E.op("vector", lambda v: v.tensor_sub(
                srow[:, 32:40], srow[:, 24:32], srow[:, 32:40]), waits=[t_m2], inc="vs")
            sdtok = E.op("scalar", lambda sc: sc.activation(
                srow[:, 40:48], srow[:, 32:40], AF.Sqrt, bias=epsb[:, 0:1]),
                waits=[vtok], inc="ss")
            rctok = E.op("vector", lambda v: v.reciprocal(srow[:, 40:48], srow[:, 40:48]),
                         waits=[sdtok], inc="vs")
            t_nb = E.op("vector", lambda v: v.tensor_mul(
                srow[:, 48:56], srow[:, 16:24], srow[:, 40:48]), waits=[rctok], inc="vs")
            nbtok = E.op("vector", lambda v: v.tensor_scalar(
                srow[:, 48:56], srow[:, 48:56], -1.0, None, OP.mult), waits=[t_nb], inc="vs")
            out_issued = [0]
            for t8 in range(8):
                out_waits = [nbtok]
                if out_issued[0] >= 4:
                    out_waits.append(("d_out", 16 * (out_issued[0] - 3)))
                otok = E.op("vector", lambda v, t8=t8: v.scalar_tensor_tensor(
                    ostage[t8 % 4][:, :], tstage[t8][:, :], srow[:, 40 + t8:41 + t8],
                    bass.AP(srow, 48 + t8, [[64, 128], [0, C]]), OP.mult, OP.add),
                    waits=out_waits, inc="vs")
                E.op("sync", lambda s, t8=t8: s.dma_start(
                    bass.AP(y, t8 * 128 * C, [[C, 128], [1, C]]), ostage[t8 % 4][:, :]),
                    waits=[otok], inc="d_out", inc_n=16)
                out_issued[0] += 1
            E.op("sync", None, waits=[("d_out", 16 * 8)])
            E.flush(nc, sems)

    return nc


_NC_CACHE = []


def kernel(L0_attn_x, L0_attn_w_q, L0_attn_w_k, L0_attn_w_v, L0_attn_w_o,
           L0_ffn_gate_w, L0_ffn_up_w, L0_ffn_down_w,
           L0_ffn_gate_scale, L0_ffn_up_scale, L0_ffn_down_scale):
    x = np.ascontiguousarray(np.asarray(L0_attn_x, dtype=np.float32))
    wq = np.asarray(L0_attn_w_q, np.float32)
    wk = np.asarray(L0_attn_w_k, np.float32)
    wv = np.asarray(L0_attn_w_v, np.float32)
    wo = np.asarray(L0_attn_w_o, np.float32)
    gw = np.ascontiguousarray(np.asarray(L0_ffn_gate_w, np.float32))
    uwv = np.ascontiguousarray(np.asarray(L0_ffn_up_w, np.float32))
    dwv = np.ascontiguousarray(np.asarray(L0_ffn_down_w, np.float32))
    gs = np.asarray(L0_ffn_gate_scale, np.float32)
    us = np.asarray(L0_ffn_up_scale, np.float32)
    ds = np.asarray(L0_ffn_down_scale, np.float32)

    wqT = np.ascontiguousarray(wq.T)
    wkT = np.ascontiguousarray(wk.T)
    wvT = np.ascontiguousarray(wv.T)
    woT = np.ascontiguousarray(wo.T)
    gst = np.ascontiguousarray(gs.reshape(NI, 128).T)
    ust = np.ascontiguousarray(us.reshape(NI, 128).T)
    dstv = np.ascontiguousarray(ds.reshape(CK, 128).T)
    ones = np.full((128, 128), 1.0, np.float32)
    ident = np.eye(128, dtype=np.float32)

    in_maps = []
    for core in range(8):
        b, qc = core // 4, core % 4
        toks, cos, ssn = _core_tables(qc)
        kbmulp, kbms, winm = _core_masks(qc)
        xT = x[b].T
        xTp = np.concatenate([xT, np.zeros((C, 1), np.float32)], axis=1)
        idx = np.where(toks < 0, T, toks)
        xkv = np.ascontiguousarray(xTp[:, idx])
        in_maps.append({
            "xkv": xkv, "wqT": wqT, "wkT": wkT, "wvT": wvT, "woT": woT,
            "gw": gw, "uw": uwv, "dw": dwv,
            "gst": gst, "ust": ust, "dst": dstv,
            "tabc": cos, "tabs": ssn,
            "kbmulp": kbmulp, "kbms": kbms, "winm": winm,
            "ones": ones, "ident": ident,
        })

    if not _NC_CACHE:
        _NC_CACHE.append(build_nc())
    nc = _NC_CACHE[0]
    res = run_bass_kernel_spmd(nc, in_maps, core_ids=list(range(8)))
    out = np.empty((2, T, C), np.float32)
    for core in range(8):
        b, qc = core // 4, core % 4
        out[b, 1024 * qc: 1024 * (qc + 1)] = res.results[core]["y"]
    return out
